# version 01 (final)
# baseline (speedup 1.0000x reference)
import contextlib
import numpy as np
import ml_dtypes

import concourse.bass as bass
import concourse.mybir as mybir
from concourse.bass_utils import run_bass_kernel_spmd
from concourse.alu_op_type import AluOpType as A

F32 = mybir.dt.float32
BF16 = mybir.dt.bfloat16
I32 = mybir.dt.int32
AF = mybir.ActivationFunctionType
BF = ml_dtypes.bfloat16

NCORE = 8
TOK = 2048
DM = 1024
NCH = 8


_NMC = [0]

def _nm(n):
    _NMC[0] += 1
    return '%s_u%d' % (n, _NMC[0])


class PB:
    def __init__(self, nc, es):
        self.nc = nc
        self.order = ['tensor', 'scalar', 'vector', 'gpsimd', 'sync']
        self.ops = {e: [] for e in self.order}
        self.sem = {}
        self.cnt = {}
        for s in ('pe', 'act', 'dve', 'pool', 'ld', 'gdma'):
            self.sem[s] = es.enter_context(nc.semaphore(s))
            self.cnt[s] = 0
        self.prod = {}
        self.readers = {}
        self.pending = {}

    def fence(self):
        snap = dict(self.cnt)
        for e in self.order:
            cur = self.pending.setdefault(e, {})
            for s, c in snap.items():
                cur[s] = max(cur.get(s, 0), c)

    def emit(self, eng, fn, reads=(), writes=(), sem=None, inc=None):
        if sem is None:
            sem = {'tensor': 'pe', 'scalar': 'act', 'vector': 'dve',
                   'gpsimd': 'pool', 'sync': 'ld'}[eng]
        if inc is None:
            inc = 16 if sem in ('ld', 'gdma') else 1
        need = {}
        if fn is not None and self.pending.get(eng):
            for s, c in self.pending.pop(eng).items():
                if s != sem or c > 0:
                    need[s] = max(need.get(s, 0), c)
        def add(mark):
            if mark:
                need[mark[0]] = max(need.get(mark[0], 0), mark[1])
        for t in reads:
            add(self.prod.get(t))
        for t in writes:
            add(self.prod.get(t))
            for m in self.readers.get(t, ()):
                add(m)
        if fn is not None and sem in ('ld', 'gdma') and self.cnt[sem] > 0:
            need[sem] = max(need.get(sem, 0), self.cnt[sem])
        waits = [(s, c) for s, c in need.items() if c > 0]
        self.ops[eng].append((waits, fn, sem, inc))
        if fn is not None:
            self.cnt[sem] += inc
            mark = (sem, self.cnt[sem])
            for t in writes:
                self.prod[t] = mark
                self.readers[t] = []
            for t in reads:
                self.readers.setdefault(t, []).append(mark)

    def wait_all(self, eng='sync'):
        waits = [(s, c) for s, c in self.cnt.items() if c > 0]
        self.ops[eng].append((waits, None, None, 0))

    def finalize(self, block):
        for eng_name in self.order:
            lst = self.ops[eng_name]
            if not lst:
                continue
            sec = getattr(block, eng_name)
            def make(lst):
                def body(E):
                    for waits, fn, sem, inc in lst:
                        for s, c in waits:
                            E.wait_ge(self.sem[s], c)
                        if fn is not None:
                            instr = fn(E)
                            instr.then_inc(self.sem[sem], inc)
                return body
            sec(make(lst))



class _scope:
    def __init__(self, pb):
        self.pb = pb
        self.es = contextlib.ExitStack()
    def __enter__(self):
        return self.es.__enter__()
    def __exit__(self, *a):
        r = self.es.__exit__(*a)
        self.pb.fence()
        return r


def dma(pb, out, in_, reads, writes, eng='sync'):
    pb.emit(eng, lambda E: E.dma_start(out=out, in_=in_), reads=reads, writes=writes,
            sem='ld' if eng == 'sync' else 'gdma', inc=16)

def mm(pb, out, lhsT, rhs, reads, writes, start=True, stop=True):
    pb.emit('tensor', lambda E: E.matmul(out, lhsT, rhs, start=start, stop=stop),
            reads=reads, writes=writes)

def transp(pb, out, in_, ident, reads, writes):
    pb.emit('tensor', lambda E: E.transpose(out, in_, ident), reads=reads, writes=writes)

def act(pb, out, in_, func, reads, writes, bias=None, scale=None, accum_out=None):
    kw = {}
    if bias is not None:
        kw['bias'] = bias
    if scale is not None:
        kw['scale'] = scale
    if accum_out is not None:
        kw['accum_out'] = accum_out
    pb.emit('scalar', lambda E: E.activation(out=out, in_=in_, func=func, **kw),
            reads=reads, writes=writes)

def scopy(pb, out, in_, reads, writes):
    pb.emit('scalar', lambda E: E.copy(out=out, in_=in_), reads=reads, writes=writes)

def vcopy(pb, out, in_, reads, writes):
    pb.emit('vector', lambda E: E.tensor_copy(out=out, in_=in_), reads=reads, writes=writes)

def tt(pb, out, in0, in1, op, reads, writes):
    pb.emit('vector', lambda E: E.tensor_tensor(out, in0, in1, op), reads=reads, writes=writes)

def ts(pb, out, in0, s1, op0, reads, writes, s2=None, op1=None):
    if s2 is None:
        pb.emit('vector', lambda E: E.tensor_scalar(out=out, in0=in0, scalar1=s1,
                                                    scalar2=None, op0=op0),
                reads=reads, writes=writes)
    else:
        pb.emit('vector', lambda E: E.tensor_scalar(out=out, in0=in0, scalar1=s1,
                                                    scalar2=s2, op0=op0, op1=op1),
                reads=reads, writes=writes)

def stt(pb, out, in0, scalar, in1, op0, op1, reads, writes):
    pb.emit('vector', lambda E: E.scalar_tensor_tensor(out=out, in0=in0, scalar=scalar,
                                                       in1=in1, op0=op0, op1=op1),
            reads=reads, writes=writes)

def vred(pb, out, in_, op, reads, writes):
    pb.emit('vector', lambda E: E.tensor_reduce(out, in_, mybir.AxisListType.X, op),
            reads=reads, writes=writes)

def vrecip(pb, out, in_, reads, writes):
    pb.emit('vector', lambda E: E.reciprocal(out, in_), reads=reads, writes=writes)

def memset(pb, ap, val, reads, writes):
    pb.emit('gpsimd', lambda E: E.memset(ap, val), reads=reads, writes=writes, sem='pool', inc=1)



def emit_cast_load(pb, dram_ap, stage, dest, tag, nbytes_tag=None):
    st_tag = tag + '_st'
    dma(pb, stage, dram_ap, reads=(), writes=(st_tag,))
    vcopy(pb, dest, stage, reads=(st_tag,), writes=(tag,))


def emit_l0ffn(pb, nc, es, T, W):
    xnb, xt = T['xnb'], T['xt']
    HID = 24
    s = 1.0 / 127.0
    heb = T['heb']
    for th in range(2):
        with _scope(pb) as ph:
            pG = [ph.enter_context(nc.psum_tensor(_nm('pG%d' % i), [128, 1024], F32))[:, :] for i in range(2)]
            pU = [ph.enter_context(nc.psum_tensor(_nm('pU%d' % i), [128, 1024], F32))[:, :] for i in range(2)]
            for hc in range(HID):
                wgs, wgb = T['wst0'], T['wbb0']
                dma(pb, wgs[:, 0:1024], W['wg'][hc], reads=('wgb0',), writes=('wgs0',))
                dma(pb, wgs[:, 1024:2048], W['wu'][hc], reads=('wgb0',), writes=('wgs0',), eng='gpsimd')
                vcopy(pb, wgb, wgs, reads=('wgs0',), writes=('wgb0',))
                g = pG[hc % 2]
                u = pU[hc % 2]
                for c in range(NCH):
                    for i in range(2):
                        sl = slice(i * 512, (i + 1) * 512)
                        rhs = xnb[:, c * TOK + th * 1024 + i * 512: c * TOK + th * 1024 + (i + 1) * 512]
                        mm(pb, g[:, sl], wgb[:, c * 128:(c + 1) * 128], rhs,
                           reads=('wgb0', 'xnb%d' % c),
                           writes=('pG%d' % (hc % 2),), start=(c == 0), stop=(c == NCH - 1))
                        mm(pb, u[:, sl], wgb[:, 1024 + c * 128: 1024 + (c + 1) * 128], rhs,
                           reads=('wgb0',), writes=('pU%d' % (hc % 2),),
                           start=(c == 0), stop=(c == NCH - 1))
                sg = T['sg0']
                act(pb, sg, g, AF.Silu, scale=s, reads=('pG%d' % (hc % 2),), writes=('sg0',))
                tt(pb, heb[:, hc * 1024:(hc + 1) * 1024], u, sg, A.mult,
                   reads=('pU%d' % (hc % 2), 'sg0'), writes=('heb%d' % hc,))
        with _scope(pb) as ph:
            pY = [ph.enter_context(nc.psum_tensor(_nm('pY%d' % i), [128, 1024], F32))[:, :] for i in range(2)]
            for dc in range(NCH):
                wds, wdb = T['wdst0'], T['wdbb0']
                dma(pb, wds, W['wd'][dc][:, 0:1536], reads=('wdb0',), writes=('wds0',))
                vcopy(pb, wdb[:, 0:1536], wds, reads=('wds0',), writes=('wdb0',))
                dma(pb, wds, W['wd'][dc][:, 1536:3072], reads=('wdb0',), writes=('wds0',))
                vcopy(pb, wdb[:, 1536:3072], wds, reads=('wds0',), writes=('wdb0',))
                y = pY[dc % 2]
                for hc in range(HID):
                    for i in range(2):
                        sl = slice(i * 512, (i + 1) * 512)
                        mm(pb, y[:, sl], wdb[:, hc * 128:(hc + 1) * 128],
                           heb[:, hc * 1024 + i * 512: hc * 1024 + (i + 1) * 512],
                           reads=('wdb0', 'heb%d' % hc), writes=('pY%d' % (dc % 2),),
                           start=(hc == 0), stop=(hc == HID - 1))
                xsl = xt[:, dc * TOK + th * 1024: dc * TOK + (th + 1) * 1024]
                stt(pb, xsl, y, 0.5 * s * s, xsl, A.mult, A.add,
                    reads=('pY%d' % (dc % 2), 'xt'), writes=('xt',))


def emit_l1scores(pb, nc, es, T, W):
    for nm in ('lq1', 'lk1', 'lq2', 'lk2'):
        dma(pb, T[nm], W[nm], reads=(), writes=(nm,))
    tt(pb, T['lrow'], T['lq1'], T['lk1'], A.mult, reads=('lq1', 'lk1'), writes=('lrow',))
    vred(pb, T['d1'], T['lrow'], A.add, reads=('lrow',), writes=('d1',))
    act(pb, T['d1'], T['d1'], AF.Exp, reads=('d1',), writes=('d1',))
    tt(pb, T['lrow'], T['lq2'], T['lk2'], A.mult, reads=('lq2', 'lk2'), writes=('lrow',))
    vred(pb, T['d2'], T['lrow'], A.add, reads=('lrow',), writes=('d2',))
    act(pb, T['d2'], T['d2'], AF.Exp, reads=('d2',), writes=('d2',))
    tt(pb, T['d1'], T['d1'], T['d2'], A.subtract, reads=('d1', 'd2'), writes=('d1',))
    ts(pb, T['d1'], T['d1'], 0.2, A.add, reads=('d1',), writes=('d1',))
    ts(pb, T['d1'], T['d1'], -1.0, A.mult, reads=('d1',), writes=('d1',))
    with _scope(pb) as ph:
        pL = ph.enter_context(nc.psum_tensor(_nm('pLam'), [128, 1], F32))[:, :]
        mm(pb, pL, T['ones1f'], T['d1'], reads=('d1', 'ones1f'), writes=('pLam',), start=True, stop=True)
        vcopy(pb, T['nlamcol'], pL, reads=('pLam',), writes=('nlamcol',))
    with _scope(pb) as ph:
        pS1 = ph.enter_context(nc.psum_tensor(_nm('pS1'), [128, 512], F32))[:, :]
        pS2 = ph.enter_context(nc.psum_tensor(_nm('pS2'), [128, 512], F32))[:, :]
        pC1 = ph.enter_context(nc.psum_tensor(_nm('pC1'), [128, 512], F32))[:, :]
        pC2 = ph.enter_context(nc.psum_tensor(_nm('pC2'), [128, 512], F32))[:, :]
        pR1 = ph.enter_context(nc.psum_tensor(_nm('pR1'), [1, 512], F32))[:, :]
        pR2 = ph.enter_context(nc.psum_tensor(_nm('pR2'), [1, 512], F32))[:, :]
        pTt = [ph.enter_context(nc.psum_tensor(_nm('pTt%d' % i), [128, 128], BF16))[:, :] for i in range(2)]
        for h in range(16):
            hb = h % 2
            k1h, k2h, vh = T['k1h%d' % hb], T['k2h%d' % hb], T['vh%d' % hb]
            q1h, q2h = T['q1h%d' % hb], T['q2h%d' % hb]
            dma(pb, k1h, W['k1f'][:, h * 4096:(h + 1) * 4096], reads=(), writes=('k1h%d' % hb,))
            dma(pb, k2h, W['k2f'][:, h * 4096:(h + 1) * 4096], reads=(), writes=('k2h%d' % hb,), eng='gpsimd')
            dma(pb, q1h, W['q1s'][:, h * TOK:(h + 1) * TOK], reads=(), writes=('q1h%d' % hb,))
            dma(pb, q2h, W['q2s'][:, h * TOK:(h + 1) * TOK], reads=(), writes=('q2h%d' % hb,), eng='gpsimd')
            dma(pb, vh, W['vf'][h * 128:(h + 1) * 128, :], reads=(), writes=('vh%d' % hb,))
            vtok = T['vtok%d' % hb]
            for kc in range(32):
                p = pTt[kc % 2]
                transp(pb, p, vh[:, kc * 128:(kc + 1) * 128], T['identb'],
                       reads=('vh%d' % hb, 'identb'), writes=('pTt%d' % (kc % 2),))
                vcopy(pb, vtok[:, kc * 128:(kc + 1) * 128], p,
                      reads=('pTt%d' % (kc % 2),), writes=('vtok%d' % hb,))
            comb = T['comb%d' % hb]
            for qq in range(4):
                qsl = slice(qq * 512, (qq + 1) * 512)
                for kc in range(32):
                    e1, e2 = T['e1_%d' % (kc % 2)], T['e2_%d' % (kc % 2)]
                    mm(pb, pS1, k1h[:, kc * 128:(kc + 1) * 128], q1h[:, qsl],
                       reads=('k1h%d' % hb, 'q1h%d' % hb), writes=('pS1',), start=True, stop=True)
                    act(pb, e1, pS1, AF.Exp, reads=('pS1',), writes=('e1_%d' % (kc % 2),))
                    mm(pb, pS2, k2h[:, kc * 128:(kc + 1) * 128], q2h[:, qsl],
                       reads=('k2h%d' % hb, 'q2h%d' % hb), writes=('pS2',), start=True, stop=True)
                    act(pb, e2, pS2, AF.Exp, reads=('pS2',), writes=('e2_%d' % (kc % 2),))
                    mm(pb, pR1, T['onesb'], e1, reads=('e1_%d' % (kc % 2), 'onesb'),
                       writes=('pR1',), start=(kc == 0), stop=(kc == 31))
                    mm(pb, pR2, T['onesb'], e2, reads=('e2_%d' % (kc % 2), 'onesb'),
                       writes=('pR2',), start=(kc == 0), stop=(kc == 31))
                    mm(pb, pC1, vtok[:, kc * 128:(kc + 1) * 128], e1,
                       reads=('vtok%d' % hb, 'e1_%d' % (kc % 2)), writes=('pC1',),
                       start=(kc == 0), stop=(kc == 31))
                    mm(pb, pC2, vtok[:, kc * 128:(kc + 1) * 128], e2,
                       reads=('vtok%d' % hb, 'e2_%d' % (kc % 2)), writes=('pC2',),
                       start=(kc == 0), stop=(kc == 31))
                vcopy(pb, T['r1f'], pR1, reads=('pR1',), writes=('r1f',))
                vrecip(pb, T['r1f'], T['r1f'], reads=('r1f',), writes=('r1f',))
                vcopy(pb, T['r1b'], T['r1f'], reads=('r1f',), writes=('r1b',))
                vcopy(pb, T['r2f'], pR2, reads=('pR2',), writes=('r2f',))
                vrecip(pb, T['r2f'], T['r2f'], reads=('r2f',), writes=('r2f',))
                vcopy(pb, T['r2b'], T['r2f'], reads=('r2f',), writes=('r2b',))
                mm(pb, pS1, T['ones1'], T['r1b'], reads=('r1b', 'ones1'), writes=('pS1',), start=True, stop=True)
                vcopy(pb, T['r1bc'], pS1, reads=('pS1',), writes=('r1bc',))
                mm(pb, pS2, T['ones1'], T['r2b'], reads=('r2b', 'ones1'), writes=('pS2',), start=True, stop=True)
                vcopy(pb, T['r2bc'], pS2, reads=('pS2',), writes=('r2bc',))
                tt(pb, T['ct1'], pC1, T['r1bc'], A.mult, reads=('pC1', 'r1bc'), writes=('ct1',))
                tt(pb, T['ct2'], pC2, T['r2bc'], A.mult, reads=('pC2', 'r2bc'), writes=('ct2',))
                stt(pb, comb[:, qq * 512:(qq + 1) * 512], T['ct2'], T['nlamcol'][:, 0:1], T['ct1'],
                    A.mult, A.add, reads=('ct2', 'nlamcol', 'ct1'), writes=('comb%d' % hb,))
            vred(pb, T['statcols'][:, h:h + 1], comb, A.add, reads=('comb%d' % hb,), writes=('statcols',))
            act(pb, T['sqscr'], comb, AF.Square, reads=('comb%d' % hb,), writes=('sqscr',),
                accum_out=T['statcols'][:, 16 + h:17 + h])
            comb3 = comb.rearrange("p (t r) -> p t r", r=128)
            for r in range(32):
                p = pTt[r % 2]
                transp(pb, p[0:64, :], comb3[:, :, slice(r, 128, 32)], T['identb'],
                       reads=('comb%d' % hb, 'identb'), writes=('pTt%d' % (r % 2),))
                vcopy(pb, T['oscb'][0:64, r * 128:(r + 1) * 128], p[0:64, :],
                      reads=('pTt%d' % (r % 2),), writes=('oscb',))
            dma(pb, W['o_osc'][h * 64:(h + 1) * 64, :], T['oscb'][0:64, :],
                reads=('oscb',), writes=('o_osc',))
    with _scope(pb) as ph:
        pSt = ph.enter_context(nc.psum_tensor(_nm('pSt'), [1, 32], F32))[:, :]
        mm(pb, pSt, T['onesbf'], T['statcols'], reads=('statcols', 'onesbf'), writes=('pSt',),
           start=True, stop=True)
        vcopy(pb, T['statrow'], pSt, reads=('pSt',), writes=('statrow',))
        dma(pb, W['o_stats'], T['statrow'], reads=('statrow',), writes=('o_stats',))


def emit_wproj(pb, nc, es, T, W):
    dma(pb, T['st2'][:, 0:32], W['statsg'][0:1, :], reads=(), writes=('st2',))
    dma(pb, T['st2'][:, 32:64], W['statsg'][1:2, :], reads=(), writes=('st2',))
    tt(pb, T['st1'], T['st2'][:, 0:32], T['st2'][:, 32:64], A.add, reads=('st2',), writes=('st1',))
    NQ = 128 * 4096
    ts(pb, T['mu'], T['st1'][:, 0:16], 1.0 / NQ, A.mult, reads=('st1',), writes=('mu',))
    ts(pb, T['msq'], T['st1'][:, 16:32], 1.0 / NQ, A.mult, reads=('st1',), writes=('msq',))
    tt(pb, T['vtmp'], T['mu'], T['mu'], A.mult, reads=('mu',), writes=('vtmp',))
    tt(pb, T['msq'], T['msq'], T['vtmp'], A.subtract, reads=('msq', 'vtmp'), writes=('msq',))
    ts(pb, T['msq'], T['msq'], 1e-5, A.add, reads=('msq',), writes=('msq',))
    act(pb, T['msq'], T['msq'], AF.Sqrt, reads=('msq',), writes=('msq',))
    vrecip(pb, T['msq'], T['msq'], reads=('msq',), writes=('msq',))
    ts(pb, T['msq'], T['msq'], 0.8, A.mult, reads=('msq',), writes=('msq',))
    dma(pb, T['al'], W['alpha'], reads=(), writes=('al',))
    ts(pb, T['al'], T['al'], 2.0, A.mult, reads=('al',), writes=('al',), s2=1.0, op1=A.add)
    ts(pb, T['al'], T['al'], 1.0 / np.pi, A.mult, reads=('al',), writes=('al',))
    dma(pb, T['mcols'], W['mcol'], reads=(), writes=('mcols',))
    with _scope(pb) as ph:
        pB = ph.enter_context(nc.psum_tensor(_nm('pBc'), [128, 33], F32))[:, :]
        mm(pb, pB[:, 0:16], T['ones1f'], T['mu'], reads=('mu', 'ones1f'), writes=('pBc',), start=True, stop=True)
        mm(pb, pB[:, 16:32], T['ones1f'], T['msq'], reads=('msq', 'ones1f'), writes=('pBc',), start=True, stop=True)
        mm(pb, pB[:, 32:33], T['ones1f'], T['al'], reads=('al', 'ones1f'), writes=('pBc',), start=True, stop=True)
        vcopy(pb, T['cols'], pB, reads=('pBc',), writes=('cols',))
    mucols = T['cols'][:, 0:16]
    sccols = T['cols'][:, 16:32]
    s1col = T['cols'][:, 32:33]
    for ec in range(16):
        oscin = T['oscin%d' % (ec % 2)]
        otag = 'oscin%d' % (ec % 2)
        dma(pb, oscin[0:64, :], W['oscg'][ec * 64:(ec + 1) * 64, :], reads=(), writes=(otag,))
        dma(pb, oscin[64:128, :], W['oscg'][1024 + ec * 64: 1024 + (ec + 1) * 64, :],
            reads=(), writes=(otag,), eng='gpsimd')
        osc3 = oscin.rearrange("p (b s) -> p b s", s=128)
        ev = osc3[:, slice(0, 32, 2), :]
        od = osc3[:, slice(1, 32, 2), :]
        pb.emit('vector', lambda E, _o=T['rtmp'], _i=ev, _s=T['mcols'][:, 0:1]:
                E.tensor_single_scalar(out=_o, in_=_i, scalar=_s, op=A.mult),
                reads=(otag, 'mcols'), writes=('rtmp',))
        stt(pb, T['rtmp2'], od, T['mcols'][:, 1:2], T['rtmp'], A.mult, A.add,
            reads=(otag, 'mcols', 'rtmp'), writes=('rtmp2',))
        pb.emit('vector', lambda E, _o=T['rtmp2'], _i=T['rtmp2'], _s=mucols[:, ec:ec + 1]:
                E.tensor_single_scalar(out=_o, in_=_i, scalar=_s, op=A.subtract),
                reads=('rtmp2', 'cols'), writes=('rtmp2',))
        pb.emit('vector', lambda E, _o=T['rhsall'][:, ec * TOK:(ec + 1) * TOK], _i=T['rtmp2'],
                _s=sccols[:, ec:ec + 1]:
                E.tensor_single_scalar(out=_o, in_=_i, scalar=_s, op=A.mult),
                reads=('rtmp2', 'cols'), writes=('rhs%d' % ec,))
        wps = T['wpst%d' % (ec % 2)]
        dma(pb, wps, W['wp'][ec], reads=(), writes=('wpst%d' % (ec % 2),))
        vcopy(pb, T['wpall'][:, ec * TOK:(ec + 1) * TOK], wps,
              reads=('wpst%d' % (ec % 2),), writes=('wp%d' % ec,))
    with _scope(pb) as ph:
        pP = [ph.enter_context(nc.psum_tensor(_nm('pPP%d' % i), [128, TOK], F32))[:, :] for i in range(2)]
        for j in range(8):
            for which in range(2):
                pc = j + 8 * which
                p = pP[which]
                for ec in range(16):
                    for i in range(4):
                        sl = slice(i * 512, (i + 1) * 512)
                        mm(pb, p[:, sl], T['wpall'][:, (ec * 16 + pc) * 128 // 16: 0] if False else
                           T['wpall'][:, ec * TOK + pc * 128: ec * TOK + (pc + 1) * 128],
                           T['rhsall'][:, ec * TOK + i * 512: ec * TOK + (i + 1) * 512],
                           reads=('wp%d' % ec, 'rhs%d' % ec), writes=('pPP%d' % which,),
                           start=(ec == 0), stop=(ec == 15))
            act(pb, T['gaf'], pP[0], AF.Arctan, reads=('pPP0',), writes=('gaf',))
            pb.emit('vector', lambda E, _o=T['gaf'], _i=T['gaf'], _s=s1col:
                    E.tensor_single_scalar(out=_o, in_=_i, scalar=_s, op=A.mult),
                    reads=('gaf', 'cols'), writes=('gaf',))
            ts(pb, T['gaf'], T['gaf'], 0.5, A.add, reads=('gaf',), writes=('gaf',))
            tt(pb, T['gvf'], T['gaf'], pP[1], A.mult, reads=('gaf', 'pPP1'), writes=('gvf',))
            xs = T['xst0']
            dma(pb, xs, W['xtin'][:, j * TOK:(j + 1) * TOK], reads=('xst0',), writes=('xst0',))
            tt(pb, xs, xs, T['gvf'], A.add, reads=('xst0', 'gvf'), writes=('xst0',))
            dma(pb, W['o_xt'][:, j * TOK:(j + 1) * TOK], xs, reads=('xst0',), writes=('o_xtw', 'xst0'))



def emit_ln2(pb, nc, T, W, src, mode='ln', normalize=True, rms_col=None):
    _ph = contextlib.ExitStack()
    pA = _ph.enter_context(nc.psum_tensor(_nm('pA'), [1, TOK], F32))[:, :]
    pB = _ph.enter_context(nc.psum_tensor(_nm('pB'), [1, TOK], F32))[:, :]
    for c in range(NCH):
        xb = T['xc0']
        xtag = 'xc0'
        if src == 'sbuf':
            scopy(pb, xb, T['xt'][:, c * TOK:(c + 1) * TOK], reads=('xt',), writes=(xtag,))
        else:
            xs = T['xst0']
            dma(pb, xs, W['xtin'][:, c * TOK:(c + 1) * TOK], reads=('xst0',), writes=('xst0',))
            scopy(pb, xb, xs, reads=('xst0',), writes=(xtag,))
        for i in range(4):
            sl = slice(i * 512, (i + 1) * 512)
            mm(pb, pA[:, sl], T['onesb'], xb[:, sl], reads=(xtag, 'onesb'), writes=('pA',),
               start=(c == 0), stop=(c == NCH - 1))
        sq = T['sq0']
        act(pb, sq, xb, AF.Square, reads=(xtag,), writes=('sq0',))
        for i in range(4):
            sl = slice(i * 512, (i + 1) * 512)
            mm(pb, pB[:, sl], T['onesb'], sq[:, sl], reads=('sq0', 'onesb'),
               writes=('pB',), start=(c == 0), stop=(c == NCH - 1))
    r_mean, r_ms = T['r_mean'], T['r_ms']
    ts(pb, r_mean, pA[0:1, :], 1.0 / DM, A.mult, reads=('pA',), writes=('r_mean',))
    ts(pb, r_ms, pB[0:1, :], 1.0 / DM, A.mult, reads=('pB',), writes=('r_ms',))
    tt(pb, pA[0:1, :], r_mean, r_mean, A.mult, reads=('r_mean',), writes=('pA',))
    tt(pb, r_ms, r_ms, pA[0:1, :], A.subtract, reads=('r_ms', 'pA'), writes=('r_ms',))
    if mode == 'ln':
        ts(pb, r_ms, r_ms, 1e-5, A.add, reads=('r_ms',), writes=('r_ms',))
    else:
        ts(pb, r_ms, r_ms, 1.0 + 1e-6, A.mult, reads=('r_ms',), writes=('r_ms',),
           s2=1e-5 * 1e-6, op1=A.add)
    act(pb, r_ms, r_ms, AF.Sqrt, reads=('r_ms',), writes=('r_ms',))
    vrecip(pb, r_ms, r_ms, reads=('r_ms',), writes=('r_ms',))
    _ph.close()
    pb.fence()
    _ph2 = contextlib.ExitStack()
    pBC = _ph2.enter_context(nc.psum_tensor(_nm('pBC'), [128, TOK], F32))[:, :]
    for nm, row in (('bc_mean', 'r_mean'), ('bc_rstd', 'r_ms')):
        for i in range(4):
            sl = slice(i * 512, (i + 1) * 512)
            mm(pb, pBC[:, sl], T['ones1f'], T[row][:, sl], reads=(row, 'ones1f'), writes=('pBC',),
               start=True, stop=True)
        vcopy(pb, T[nm], pBC, reads=('pBC',), writes=(nm,))
    _ph2.close()
    pb.fence()
    if normalize:
        xnb = T['xnb']
        for c in range(NCH):
            xc = xnb[:, c * TOK:(c + 1) * TOK]
            if src == 'sbuf':
                scopy(pb, xc, T['xt'][:, c * TOK:(c + 1) * TOK], reads=('xt',), writes=('xnb%d' % c,))
            else:
                xs = T['xst0']
                dma(pb, xs, W['xtin'][:, c * TOK:(c + 1) * TOK], reads=('xst0',), writes=('xst0',))
                scopy(pb, xc, xs, reads=('xst0',), writes=('xnb%d' % c,))
            tt(pb, xc, xc, T['bc_mean'], A.subtract, reads=('xnb%d' % c, 'bc_mean'), writes=('xnb%d' % c,))
            tt(pb, xc, xc, T['bc_rstd'], A.mult, reads=('xnb%d' % c, 'bc_rstd'), writes=('xnb%d' % c,))
            if rms_col is not None:
                pb.emit('vector', lambda E, _o=xc, _s=rms_col[:, c:c + 1]:
                        E.tensor_single_scalar(out=_o, in_=_o, scalar=_s, op=A.mult),
                        reads=('xnb%d' % c, 'rmsw'), writes=('xnb%d' % c,))



def _sb(es, nc, name, shape, dt):
    return es.enter_context(nc.sbuf_tensor(_nm(name), shape, dt))[:, :]

def _ps(es, nc, name, shape, dt=F32):
    return es.enter_context(nc.psum_tensor(_nm(name), shape, dt))[:, :]


def _ln_tiles(es, nc, T, with_xnb=True, dram_src=False):
    T['sq0'] = _sb(es, nc, 'sq0', [128, TOK], BF16)
    T['xc0'] = _sb(es, nc, 'xc0', [128, TOK], BF16)
    T['onesb'] = _sb(es, nc, 'onesb', [128, 1], BF16)
    T['ones1'] = _sb(es, nc, 'ones1', [1, 128], BF16)
    T['ones1f'] = _sb(es, nc, 'ones1f', [1, 128], F32)
    for nm in ('r_mean', 'r_ms'):
        T[nm] = _sb(es, nc, nm, [1, TOK], F32)
    T['bc_mean'] = _sb(es, nc, 'bc_mean', [128, TOK], BF16)
    T['bc_rstd'] = _sb(es, nc, 'bc_rstd', [128, TOK], BF16)
    if with_xnb:
        T['xnb'] = _sb(es, nc, 'xnb', [128, NCH * TOK], BF16)
    if dram_src:
        T['xst0'] = _sb(es, nc, 'xst0', [128, TOK], F32)


def _init_ones(pb, T):
    memset(pb, T['onesb'][:], 1.0, reads=(), writes=('onesb',))
    memset(pb, T['ones1'][:], 1.0, reads=(), writes=('ones1',))
    memset(pb, T['ones1f'][:], 1.0, reads=(), writes=('ones1f',))


def build_l0ffn(with_ckv):
    nc = bass.Bass("TRN2", target_bir_lowering=False)
    xtin = nc.dram_tensor("xtin", [128, NCH * TOK], F32, kind="ExternalInput").ap()
    wg = nc.dram_tensor("wg", [24, 128, 1024], I32, kind="ExternalInput").ap()
    wu = nc.dram_tensor("wu", [24, 128, 1024], I32, kind="ExternalInput").ap()
    wd = nc.dram_tensor("wd", [8, 128, 3072], I32, kind="ExternalInput").ap()
    o_xt = nc.dram_tensor("o_xt", [128, NCH * TOK], F32, kind="ExternalOutput").ap()
    W = {'wg': [wg[i] for i in range(24)], 'wu': [wu[i] for i in range(24)],
         'wd': [wd[i] for i in range(8)]}
    if with_ckv:
        wdkvT = nc.dram_tensor("wdkvT", [128, 1024], F32, kind="ExternalInput").ap()
        o_ckv = nc.dram_tensor("o_ckv", [128, TOK], BF16, kind="ExternalOutput").ap()
    with contextlib.ExitStack() as es:
        pb = PB(nc, es)
        T = {}
        T['xt'] = _sb(es, nc, 'xt', [128, NCH * TOK], F32)
        _ln_tiles(es, nc, T)
        with nc.Block() as block:
            _init_ones(pb, T)
            for c in range(NCH):
                dma(pb, T['xt'][:, c * TOK:(c + 1) * TOK], xtin[:, c * TOK:(c + 1) * TOK],
                    reads=(), writes=('xt',))
            if True:
                emit_ln2(pb, nc, T, {}, 'sbuf', 'ln', True)
            with _scope(pb) as fs:
                T['heb'] = _sb(fs, nc, 'heb', [128, 24 * 1024], BF16)
                T['wst0'] = _sb(fs, nc, 'wst0', [128, 2048], I32)
                T['wbb0'] = _sb(fs, nc, 'wbb0', [128, 2048], BF16)
                T['wdst0'] = _sb(fs, nc, 'wdst0', [128, 1536], I32)
                T['wdbb0'] = _sb(fs, nc, 'wdbb0', [128, 3072], BF16)
                T['sg0'] = _sb(fs, nc, 'sg0', [128, 1024], BF16)
                emit_l0ffn(pb, nc, es, T, W)
            if with_ckv:
                if True:
                    emit_ln2(pb, nc, T, {}, 'sbuf', 'ln', True)
                with _scope(pb) as ph:
                    pK = _ps(ph, nc, 'pCKV', [128, TOK])
                    st = _sb(ph, nc, 'wdkvst', [128, 1024], F32)
                    wb = _sb(ph, nc, 'wdkvb', [128, 1024], BF16)
                    ckvb = _sb(ph, nc, 'ckvb', [128, TOK], BF16)
                    dma(pb, st, wdkvT, reads=(), writes=('wdkvst',))
                    vcopy(pb, wb, st, reads=('wdkvst',), writes=('wdkvb',))
                    for c in range(NCH):
                        for i in range(4):
                            sl = slice(i * 512, (i + 1) * 512)
                            mm(pb, pK[:, sl], wb[:, c * 128:(c + 1) * 128],
                               T['xnb'][:, c * TOK + i * 512: c * TOK + (i + 1) * 512],
                               reads=('wdkvb', 'xnb%d' % c), writes=('pCKV',),
                               start=(c == 0), stop=(c == NCH - 1))
                    vcopy(pb, ckvb, pK, reads=('pCKV',), writes=('ckvb',))
                    dma(pb, o_ckv, ckvb, reads=('ckvb',), writes=('o_ckv',))
            for c in range(NCH):
                dma(pb, o_xt[:, c * TOK:(c + 1) * TOK], T['xt'][:, c * TOK:(c + 1) * TOK],
                    reads=('xt',), writes=('o_xt',))
            pb.wait_all()
            pb.finalize(block)
    return nc


def build_l1ffn():
    nc = bass.Bass("TRN2", target_bir_lowering=False)
    xtin = nc.dram_tensor("xtin", [128, NCH * TOK], F32, kind="ExternalInput").ap()
    w1 = nc.dram_tensor("w1", [256, 128, 1024], F32, kind="ExternalInput").ap()
    w2 = nc.dram_tensor("w2", [64, 128, 4096], F32, kind="ExternalInput").ap()
    gwT = nc.dram_tensor("gwT", [128, 64], F32, kind="ExternalInput").ap()
    b1s_d = nc.dram_tensor("b1s", [128, 256], F32, kind="ExternalInput").ap()
    b2s_d = nc.dram_tensor("b2s", [128, 64], F32, kind="ExternalInput").ap()
    rmsw_d = nc.dram_tensor("rmsw", [128, 8], F32, kind="ExternalInput").ap()
    identf_d = nc.dram_tensor("identf", [128, 128], F32, kind="ExternalInput").ap()
    o_xt = nc.dram_tensor("o_xt", [128, NCH * TOK], F32, kind="ExternalOutput").ap()
    with contextlib.ExitStack() as es:
        pb = PB(nc, es)
        T = {}
        _ln_tiles(es, nc, T, dram_src=True)
        T['heb'] = _sb(es, nc, 'heb', [128, 32 * 1024], BF16)
        T['yacc'] = _sb(es, nc, 'yacc', [128, 8 * 1024], BF16)
        T['wst0'] = _sb(es, nc, 'wst0', [128, 1024], F32)
        T['wst1'] = _sb(es, nc, 'wst1', [128, 1024], F32)
        T['wbb0'] = _sb(es, nc, 'wbb0', [128, 1024], BF16)
        T['wbb1'] = _sb(es, nc, 'wbb1', [128, 1024], BF16)
        T['wdst0'] = _sb(es, nc, 'wdst0', [128, 2048], F32)
        T['wdbb0'] = _sb(es, nc, 'wdbb0', [128, 4096], BF16)
        T['wrow'] = _sb(es, nc, 'wrow', [8, TOK], F32)
        T['wr0e'] = _sb(es, nc, 'wr0e', [1, 1024], F32)
        T['identf'] = _sb(es, nc, 'identf', [128, 128], F32)
        T['rmsw'] = _sb(es, nc, 'rmsw', [128, 8], F32)
        T['b1s'] = _sb(es, nc, 'b1s', [128, 256], F32)
        T['b2s'] = _sb(es, nc, 'b2s', [128, 64], F32)
        W = {'w1': {}, 'w2': {}, 'gwT': gwT, 'b1s': T['b1s'], 'b2s': T['b2s'], 'xtin': xtin}
        for e in range(8):
            for fc in range(32):
                W['w1'][(e, fc)] = w1[e * 32 + fc]
            for dc in range(8):
                W['w2'][(e, dc)] = w2[e * 8 + dc]
        with nc.Block() as block:
            _init_ones(pb, T)
            dma(pb, T['rmsw'], rmsw_d, reads=(), writes=('rmsw',))
            dma(pb, T['b1s'], b1s_d, reads=(), writes=('b1s',))
            dma(pb, T['b2s'], b2s_d, reads=(), writes=('b2s',))
            dma(pb, T['identf'], identf_d, reads=(), writes=('identf',))
            if True:
                emit_ln2(pb, nc, T, W, 'dram', 'ln_rms', True, rms_col=T['rmsw'])
            emit_l1ffn_body(pb, nc, es, T, W, xtin, o_xt)
            pb.wait_all()
            pb.finalize(block)
    return nc


def emit_l1ffn_body(pb, nc, es, T, W, xtin, o_xt):
    xnb = T['xnb']
    E_, FC, DC = 8, 32, 8
    gs = contextlib.ExitStack()
    T['gwst'] = _sb(gs, nc, 'gwst', [128, 64], F32)
    T['gwb'] = _sb(gs, nc, 'gwb', [128, 64], BF16)
    T['slog'] = _sb(gs, nc, 'slog', [8, TOK], F32)
    T['ltok'] = _sb(gs, nc, 'ltok', [128, 8], F32)
    T['etok'] = _sb(gs, nc, 'etok', [128, 8], F32)
    T['ptok'] = _sb(gs, nc, 'ptok', [128, 8], F32)
    T['iseq'] = _sb(gs, nc, 'iseq', [128, 8], F32)
    T['wtok'] = _sb(gs, nc, 'wtok', [128, 8], F32)
    T['mx'] = _sb(gs, nc, 'mx', [128, 1], F32)
    T['sm'] = _sb(gs, nc, 'sm', [128, 1], F32)
    with _scope(pb) as ph:
        pL = _ps(ph, nc, 'pL', [8, TOK])
        emit_cast_load(pb, W['gwT'], T['gwst'], T['gwb'], 'gwb')
        for c in range(NCH):
            for i in range(4):
                sl = slice(i * 512, (i + 1) * 512)
                mm(pb, pL[:, sl], T['gwb'][:, c * 8:(c + 1) * 8],
                   xnb[:, c * TOK + i * 512: c * TOK + (i + 1) * 512],
                   reads=('gwb', 'xnb%d' % c), writes=('pL',), start=(c == 0), stop=(c == NCH - 1))
        vcopy(pb, T['slog'], pL, reads=('pL',), writes=('slog',))
    with _scope(pb) as ph:
        pT1 = _ps(ph, nc, 'pT1', [128, 8])
        pT2 = _ps(ph, nc, 'pT2', [8, 128])
        for tti in range(16):
            sl = slice(tti * 128, (tti + 1) * 128)
            transp(pb, pT1, T['slog'][:, sl], T['identf'][0:8, 0:8],
                   reads=('slog', 'identf'), writes=('pT1',))
            lt = T['ltok']
            vcopy(pb, lt, pT1, reads=('pT1',), writes=('ltok',))
            vred(pb, T['mx'], lt, A.max, reads=('ltok',), writes=('mx',))
            ts(pb, T['mx'], T['mx'], -1.0, A.mult, reads=('mx',), writes=('mx',))
            act(pb, T['etok'], lt, AF.Exp, bias=T['mx'][:, 0:1], reads=('ltok', 'mx'), writes=('etok',))
            vred(pb, T['sm'], T['etok'], A.add, reads=('etok',), writes=('sm',))
            vrecip(pb, T['sm'], T['sm'], reads=('sm',), writes=('sm',))
            pb.emit('vector', lambda E, _o=T['ptok'], _i=T['etok'], _s=T['sm'][:, 0:1]:
                    E.tensor_single_scalar(out=_o, in_=_i, scalar=_s, op=A.mult),
                    reads=('etok', 'sm'), writes=('ptok',))
            vred(pb, T['mx'], T['ptok'], A.max, reads=('ptok',), writes=('mx',))
            pb.emit('vector', lambda E, _o=T['iseq'], _i=T['ptok'], _s=T['mx'][:, 0:1]:
                    E.tensor_single_scalar(out=_o, in_=_i, scalar=_s, op=A.is_equal),
                    reads=('ptok', 'mx'), writes=('iseq',))
            tt(pb, T['wtok'], T['iseq'], T['ptok'], A.mult, reads=('iseq', 'ptok'), writes=('wtok',))
            transp(pb, pT2, T['wtok'], T['identf'], reads=('wtok', 'identf'), writes=('pT2',))
            vcopy(pb, T['wrow'][:, sl], pT2, reads=('pT2',), writes=('wrow',))
    gs.close()
    pb.fence()
    T['webc'] = _sb(es, nc, 'webc', [128, 1024], BF16)
    T['yt0'] = _sb(es, nc, 'yt0', [128, 1024], BF16)
    T['yt1'] = _sb(es, nc, 'yt1', [128, 1024], BF16)
    for th in range(2):
        for dc in range(DC):
            memset(pb, T['yacc'][:, dc * 1024:(dc + 1) * 1024], 0.0,
                   reads=('yacc',), writes=('yacc',))
        for e in range(E_):
            dma(pb, T['wr0e'], T['wrow'][e:e + 1, th * 1024:(th + 1) * 1024],
                reads=('wrow', 'wr0e'), writes=('wr0e',))
            with _scope(pb) as ph:
                pW = _ps(ph, nc, 'pW', [128, 1024])
                for i in range(2):
                    sl = slice(i * 512, (i + 1) * 512)
                    mm(pb, pW[:, sl], T['ones1f'], T['wr0e'][:, sl],
                       reads=('wr0e', 'ones1f'), writes=('pW',), start=True, stop=True)
                vcopy(pb, T['webc'], pW, reads=('pW',), writes=('webc',))
            with _scope(pb) as ph:
                pG = [_ps(ph, nc, 'pG%d' % i, [128, 1024]) for i in range(2)]
                for fc in range(FC):
                    w1s, w1b = T['wst%d' % (fc % 2)], T['wbb%d' % (fc % 2)]
                    dma(pb, w1s, W['w1'][(e, fc)], reads=(), writes=('wgs%d' % (fc % 2),))
                    vcopy(pb, w1b, w1s, reads=('wgs%d' % (fc % 2),), writes=('wgb%d' % (fc % 2),))
                    g = pG[fc % 2]
                    for c in range(NCH):
                        for i in range(2):
                            sl = slice(i * 512, (i + 1) * 512)
                            mm(pb, g[:, sl], w1b[:, c * 128:(c + 1) * 128],
                               xnb[:, c * TOK + th * 1024 + i * 512: c * TOK + th * 1024 + (i + 1) * 512],
                               reads=('wgb%d' % (fc % 2), 'xnb%d' % c), writes=('pG%d' % (fc % 2),),
                               start=(c == 0), stop=(c == NCH - 1))
                    act(pb, T['heb'][:, fc * 1024:(fc + 1) * 1024], g, AF.Silu,
                        bias=W['b1s'][:, e * FC + fc: e * FC + fc + 1],
                        reads=('pG%d' % (fc % 2), 'b1s'), writes=('heb%d' % fc,))
            with _scope(pb) as ph:
                pY = [_ps(ph, nc, 'pY%d' % i, [128, 1024]) for i in range(2)]
                for dc in range(DC):
                    w2s, w2b = T['wdst0'], T['wdbb0']
                    dma(pb, w2s, W['w2'][(e, dc)][:, 0:2048], reads=(), writes=('wds0',))
                    vcopy(pb, w2b[:, 0:2048], w2s, reads=('wds0',), writes=('wdb0',))
                    dma(pb, w2s, W['w2'][(e, dc)][:, 2048:4096], reads=('wdb0',), writes=('wds0',))
                    vcopy(pb, w2b[:, 2048:4096], w2s, reads=('wds0',), writes=('wdb0',))
                    y = pY[dc % 2]
                    for fc in range(FC):
                        for i in range(2):
                            sl = slice(i * 512, (i + 1) * 512)
                            mm(pb, y[:, sl], w2b[:, fc * 128:(fc + 1) * 128],
                               T['heb'][:, fc * 1024 + i * 512: fc * 1024 + (i + 1) * 512],
                               reads=('wdb0', 'heb%d' % fc), writes=('pY%d' % (dc % 2),),
                               start=(fc == 0), stop=(fc == FC - 1))
                    yt = T['yt%d' % (dc % 2)]
                    stt(pb, yt, y, W['b2s'][:, e * DC + dc: e * DC + dc + 1], T['webc'],
                        A.add, A.mult, reads=('pY%d' % (dc % 2), 'b2s', 'webc'), writes=('yt%d' % (dc % 2),))
                    ysl = T['yacc'][:, dc * 1024:(dc + 1) * 1024]
                    tt(pb, ysl, ysl, yt, A.add, reads=('yacc', 'yt%d' % (dc % 2)), writes=('yacc',))
        for dc in range(DC):
            xs = T['xst0'][:, 0:1024]
            dma(pb, xs, xtin[:, dc * TOK + th * 1024: dc * TOK + (th + 1) * 1024],
                reads=('xst0',), writes=('xst0',))
            stt(pb, xs, T['yacc'][:, dc * 1024:(dc + 1) * 1024], 0.5, xs, A.mult, A.add,
                reads=('yacc', 'xst0'), writes=('xst0',))
            dma(pb, o_xt[:, dc * TOK + th * 1024: dc * TOK + (th + 1) * 1024], xs,
                reads=('xst0',), writes=('o_xtw', 'xst0'))


def build_l0attn():
    nc = bass.Bass("TRN2", target_bir_lowering=False)
    xtin = nc.dram_tensor("xtin", [128, NCH * TOK], F32, kind="ExternalInput").ap()
    ckvf_d = nc.dram_tensor("ckvf", [128, 4096], BF16, kind="ExternalInput").ap()
    masks_d = nc.dram_tensor("masks", [256, 128], BF16, kind="ExternalInput").ap()
    identb_d = nc.dram_tensor("identb", [128, 128], BF16, kind="ExternalInput").ap()
    wdqT_d = nc.dram_tensor("wdqT", [128, 1024], F32, kind="ExternalInput").ap()
    wuqT_d = nc.dram_tensor("wuqT", [128, 1024], F32, kind="ExternalInput").ap()
    wdq_d = nc.dram_tensor("wdq", [128, 1024], F32, kind="ExternalInput").ap()
    wuk_d = nc.dram_tensor("wuk", [128, 1024], F32, kind="ExternalInput").ap()
    wuv_d = nc.dram_tensor("wuv", [128, 1024], F32, kind="ExternalInput").ap()
    woT_d = nc.dram_tensor("woT", [128, 8192], F32, kind="ExternalInput").ap()
    o_xt = nc.dram_tensor("o_xt", [128, NCH * TOK], F32, kind="ExternalOutput").ap()
    with contextlib.ExitStack() as es:
        pb = PB(nc, es)
        T = {}
        T['xt'] = _sb(es, nc, 'xt', [128, NCH * TOK], F32)
        T['qlb'] = _sb(es, nc, 'qlb', [128, 16 * TOK], BF16)
        T['ckvf'] = _sb(es, nc, 'ckvf', [128, 4096], BF16)
        T['keffb'] = _sb(es, nc, 'keffb', [128, 1024], BF16)
        T['keffacc'] = _sb(es, nc, 'keffacc', [128, 1024], F32)
        T['veffb'] = _sb(es, nc, 'veffb', [128, 1024], BF16)
        T['identb'] = _sb(es, nc, 'identb', [128, 128], BF16)
        T['maskA'] = _sb(es, nc, 'maskA', [128, 128], BF16)
        T['maskB'] = _sb(es, nc, 'maskB', [128, 128], BF16)
        _ln_tiles(es, nc, T, with_xnb=False)
        with nc.Block() as block:
            _init_ones(pb, T)
            for c in range(NCH):
                dma(pb, T['xt'][:, c * TOK:(c + 1) * TOK], xtin[:, c * TOK:(c + 1) * TOK],
                    reads=(), writes=('xt',))
            dma(pb, T['identb'], identb_d, reads=(), writes=('identb',))
            dma(pb, T['maskA'], masks_d[0:128, :], reads=(), writes=('masks',))
            dma(pb, T['maskB'], masks_d[128:256, :], reads=(), writes=('masks',))
            dma(pb, T['ckvf'], ckvf_d, reads=(), writes=('ckvf',))
            if True:
                emit_ln2(pb, nc, T, {}, 'sbuf', 'ln', normalize=False)
            with _scope(pb) as sw:
                wst = [_sb(sw, nc, 'wsmst0', [128, 1024], F32)]
                wst.append(wst[0])
                wsb = {}
                for nm, d_ in (('wdqT', wdqT_d), ('wuqT', wuqT_d), ('wdq', wdq_d),
                               ('wuk', wuk_d), ('wuv', wuv_d)):
                    wsb[nm] = _sb(sw, nc, nm + 'b', [128, 1024], BF16)
                    dma(pb, wst[0], d_, reads=('wsm0r',), writes=('wsm0',))
                    vcopy(pb, wsb[nm], wst[0], reads=('wsm0',), writes=(nm + 'b', 'wsm0r'))
                gkc = _sb(sw, nc, 'gkc', [128, 1024], BF16)
                memset(pb, T['keffacc'][:], 0.0, reads=(), writes=('keffacc',))
                with _scope(pb) as ph:
                    pG = _ps(ph, nc, 'pGk', [128, 1024])
                    pM = [_ps(ph, nc, 'pM%d' % i, [128, 128]) for i in range(2)]
                    for kc in range(NCH):
                        for i in range(2):
                            sl = slice(i * 512, (i + 1) * 512)
                            mm(pb, pG[:, sl], wsb['wuqT'][:, kc * 128:(kc + 1) * 128],
                               wsb['wdq'][:, i * 512:(i + 1) * 512],
                               reads=('wuqTb', 'wdqb'), writes=('pGk',), start=True, stop=True)
                        vcopy(pb, gkc, pG, reads=('pGk',), writes=('gkc',))
                        for i in range(NCH):
                            pm = pM[i % 2]
                            mm(pb, pm, gkc[:, i * 128:(i + 1) * 128], wsb['wuk'][:, kc * 128:(kc + 1) * 128],
                               reads=('gkc', 'wukb'), writes=('pM%d' % (i % 2),), start=True, stop=True)
                            ka = T['keffacc'][:, i * 128:(i + 1) * 128]
                            tt(pb, ka, ka, pm, A.add, reads=('keffacc', 'pM%d' % (i % 2)), writes=('keffacc',))
                ts(pb, T['keffb'], T['keffacc'], 0.125, A.mult, reads=('keffacc',), writes=('keffb',))
                with _scope(pb) as ph:
                    pB2 = _ps(ph, nc, 'pB2', [128, 1024])
                    for c in range(NCH):
                        dma(pb, wst[0], woT_d[:, c * 1024:(c + 1) * 1024], reads=('wsm0r',), writes=('wsm0',))
                        vcopy(pb, gkc, wst[0], reads=('wsm0',), writes=('gkc', 'wsm0r'))
                        for i in range(2):
                            sl = slice(i * 512, (i + 1) * 512)
                            mm(pb, pB2[:, sl], wsb['wuv'][:, c * 128:(c + 1) * 128], gkc[:, sl],
                               reads=('wuvb', 'gkc'), writes=('pB2',), start=(c == 0), stop=(c == NCH - 1))
                    vcopy(pb, T['veffb'], pB2, reads=('pB2',), writes=('veffb',))
                q1b = _sb(sw, nc, 'q1b', [128, TOK], BF16)
                with _scope(pb) as ph:
                    pQ1 = _ps(ph, nc, 'pQ1', [128, TOK])
                    for c in range(NCH):
                        xnc = T['xc0']
                        xtag = 'xc0'
                        scopy(pb, xnc, T['xt'][:, c * TOK:(c + 1) * TOK], reads=('xt',), writes=(xtag,))
                        tt(pb, xnc, xnc, T['bc_mean'], A.subtract, reads=(xtag, 'bc_mean'), writes=(xtag,))
                        tt(pb, xnc, xnc, T['bc_rstd'], A.mult, reads=(xtag, 'bc_rstd'), writes=(xtag,))
                        for i in range(4):
                            sl = slice(i * 512, (i + 1) * 512)
                            mm(pb, pQ1[:, sl], wsb['wdqT'][:, c * 128:(c + 1) * 128], xnc[:, sl],
                               reads=('wdqTb', xtag), writes=('pQ1',), start=(c == 0), stop=(c == NCH - 1))
                    vcopy(pb, q1b, pQ1, reads=('pQ1',), writes=('q1b',))
                with _scope(pb) as ph:
                    pQ = [_ps(ph, nc, 'pQc%d' % i, [128, TOK]) for i in range(2)]
                    qbuf = [_sb(sw, nc, 'qbuf0', [128, TOK], BF16)]
                    qbuf.append(qbuf[0])
                    for i in range(NCH):
                        p = pQ[i % 2]
                        for j in range(4):
                            sl = slice(j * 512, (j + 1) * 512)
                            mm(pb, p[:, sl], wsb['wuqT'][:, i * 128:(i + 1) * 128], q1b[:, sl],
                               reads=('wuqTb', 'q1b'), writes=('pQc%d' % (i % 2),), start=True, stop=True)
                        vcopy(pb, qbuf[0], p, reads=('pQc%d' % (i % 2),), writes=('qbuf0',))
                        for half in range(2):
                            h = 2 * i + half
                            pq = pQ[(i + 1) % 2]
                            for j in range(4):
                                sl = slice(j * 512, (j + 1) * 512)
                                mm(pb, pq[:, sl], T['keffb'][half * 64:half * 64 + 64, i * 128:(i + 1) * 128],
                                   qbuf[0][half * 64:half * 64 + 64, sl],
                                   reads=('keffb', 'qbuf0'), writes=('pQc%d' % ((i + 1) % 2),),
                                   start=True, stop=True)
                            vcopy(pb, T['qlb'][:, h * TOK:(h + 1) * TOK], pq,
                                  reads=('pQc%d' % ((i + 1) % 2),), writes=('qlb%d' % h,))
            T['ckvtok'] = _sb(es, nc, 'ckvtok', [128, 4096], BF16)
            T['eb0'] = _sb(es, nc, 'eb0', [128, 512], BF16)
            T['eb1'] = _sb(es, nc, 'eb1', [128, 512], BF16)
            T['rsf'] = _sb(es, nc, 'rsf', [1, 512], F32)
            T['rsb'] = _sb(es, nc, 'rsb', [1, 512], BF16)
            T['rbcb'] = _sb(es, nc, 'rbcb', [128, 512], BF16)
            T['ctxn'] = _sb(es, nc, 'ctxn', [128, 512], BF16)
            with _scope(pb) as ph:
                pT_ = [_ps(ph, nc, 'pTt%d' % i, [128, 128], BF16) for i in range(2)]
                for kc in range(32):
                    p = pT_[kc % 2]
                    transp(pb, p, T['ckvf'][:, kc * 128:(kc + 1) * 128], T['identb'],
                           reads=('ckvf', 'identb'), writes=('pTt%d' % (kc % 2),))
                    vcopy(pb, T['ckvtok'][:, kc * 128:(kc + 1) * 128], p,
                          reads=('pTt%d' % (kc % 2),), writes=('ckvtok',))
            with _scope(pb) as ph:
                pS = [_ps(ph, nc, 'pS%d' % i, [128, 512]) for i in range(2)]
                pC = [_ps(ph, nc, 'pC%d' % i, [128, 512]) for i in range(2)]
                pRS = _ps(ph, nc, 'pRS', [1, 512])
                pY = _ps(ph, nc, 'pYa', [128, 512])
                for qq in range(4):
                    for h in range(16):
                        pc = pC[h % 2]
                        ctag = 'pC%d' % (h % 2)
                        tiles = [(p_, t) for p_ in range(2) for t in range(4 * qq + 4)]
                        first = True
                        for (p_, t) in tiles:
                            kc = p_ * 16 + t
                            jd = t - 4 * qq
                            vs = max(0, jd) * 128
                            ps = pS[kc % 2]
                            stag = 'pS%d' % (kc % 2)
                            etile = T['eb%d' % (kc % 2)]
                            etag = 'eb%d' % (kc % 2)
                            mm(pb, ps[:, vs:512], T['ckvf'][:, kc * 128:(kc + 1) * 128],
                               T['qlb'][:, h * TOK + qq * 512 + vs: h * TOK + (qq + 1) * 512],
                               reads=('ckvf', 'qlb%d' % h), writes=(stag,), start=True, stop=True)
                            act(pb, etile[:, vs:512], ps[:, vs:512], AF.Exp, reads=(stag,), writes=(etag,))
                            if 0 <= jd < 4:
                                msk = T['maskA'] if p_ == 0 else T['maskB']
                                tt(pb, etile[:, jd * 128:(jd + 1) * 128], etile[:, jd * 128:(jd + 1) * 128],
                                   msk, A.mult, reads=(etag, 'masks'), writes=(etag,))
                            last = (p_ == 1 and t == 4 * qq + 3)
                            mm(pb, pRS[:, vs:512], T['onesb'], etile[:, vs:512],
                               reads=(etag, 'onesb'), writes=('pRS',), start=first, stop=last)
                            mm(pb, pc[:, vs:512], T['ckvtok'][:, kc * 128:(kc + 1) * 128], etile[:, vs:512],
                               reads=('ckvtok', etag), writes=(ctag,), start=first, stop=last)
                            first = False
                        vcopy(pb, T['rsf'], pRS, reads=('pRS',), writes=('rsf',))
                        vrecip(pb, T['rsf'], T['rsf'], reads=('rsf',), writes=('rsf',))
                        vcopy(pb, T['rsb'], T['rsf'], reads=('rsf',), writes=('rsb',))
                        ps = pS[0]
                        mm(pb, ps, T['ones1'], T['rsb'], reads=('rsb', 'ones1'), writes=('pS0',),
                           start=True, stop=True)
                        vcopy(pb, T['rbcb'], ps, reads=('pS0',), writes=('rbcb',))
                        tt(pb, T['ctxn'], pc, T['rbcb'], A.mult, reads=(ctag, 'rbcb'), writes=('ctxn',))
                        half = h % 2
                        mm(pb, pY[half * 64:half * 64 + 64, :], T['veffb'][:, h * 64:(h + 1) * 64],
                           T['ctxn'], reads=('veffb', 'ctxn'), writes=('pYa',), start=True, stop=True)
                        if half == 1:
                            i = h // 2
                            xsl = T['xt'][:, i * TOK + qq * 512: i * TOK + (qq + 1) * 512]
                            tt(pb, xsl, xsl, pY, A.add, reads=('xt', 'pYa'), writes=('xt',))
            for c in range(NCH):
                dma(pb, o_xt[:, c * TOK:(c + 1) * TOK], T['xt'][:, c * TOK:(c + 1) * TOK],
                    reads=('xt',), writes=('o_xt',))
            pb.wait_all()
            pb.finalize(block)
    return nc


def build_l1qkv():
    nc = bass.Bass("TRN2", target_bir_lowering=False)
    xtin = nc.dram_tensor("xtin", [128, NCH * TOK], F32, kind="ExternalInput").ap()
    wq = nc.dram_tensor("wq", [32, 128, 256], F32, kind="ExternalInput").ap()
    wk = nc.dram_tensor("wk", [32, 128, 256], F32, kind="ExternalInput").ap()
    wv = nc.dram_tensor("wv", [16, 128, 1024], F32, kind="ExternalInput").ap()
    ropes = nc.dram_tensor("ropes", [4 * 32, 2048], BF16, kind="ExternalInput").ap()
    p32_d = nc.dram_tensor("p32", [32, 32], BF16, kind="ExternalInput").ap()
    o_q1s = nc.dram_tensor("o_q1s", [32, 16 * TOK], BF16, kind="ExternalOutput").ap()
    o_q2s = nc.dram_tensor("o_q2s", [32, 16 * TOK], BF16, kind="ExternalOutput").ap()
    o_k1s = nc.dram_tensor("o_k1s", [32, 16 * TOK], BF16, kind="ExternalOutput").ap()
    o_k2s = nc.dram_tensor("o_k2s", [32, 16 * TOK], BF16, kind="ExternalOutput").ap()
    o_vt = nc.dram_tensor("o_vt", [128, 16 * TOK], BF16, kind="ExternalOutput").ap()
    outs = {'q1': o_q1s, 'q2': o_q2s, 'k1': o_k1s, 'k2': o_k2s}
    with contextlib.ExitStack() as es:
        pb = PB(nc, es)
        T = {}
        _ln_tiles(es, nc, T, dram_src=True)
        W = {'xtin': xtin}
        tabs = {}
        for i, nm in enumerate(('cosQ', 'sinQ', 'cosK', 'sinK')):
            tabs[nm] = _sb(es, nc, nm, [32, 2048], BF16)
        p32 = _sb(es, nc, 'p32t', [32, 32], BF16)
        wqst = [_sb(es, nc, 'wqst%d' % i, [128, 256], F32) for i in range(2)]
        wqb = [_sb(es, nc, 'wqb%d' % i, [128, 256], BF16) for i in range(2)]
        raw = [_sb(es, nc, 'raw%d' % i, [32, 2048], BF16) for i in range(2)]
        rt = [_sb(es, nc, 'rt%d' % i, [32, 2048], BF16) for i in range(2)]
        ydst = [_sb(es, nc, 'ydst%d' % i, [32, 2048], BF16) for i in range(2)]
        wvst = [_sb(es, nc, 'wvst%d' % i, [128, 1024], F32) for i in range(2)]
        wvb = [_sb(es, nc, 'wvb%d' % i, [128, 1024], BF16) for i in range(2)]
        vout = [_sb(es, nc, 'vout%d' % i, [128, 2048], BF16) for i in range(2)]
        with nc.Block() as block:
            _init_ones(pb, T)
            for i, nm in enumerate(('cosQ', 'sinQ', 'cosK', 'sinK')):
                dma(pb, tabs[nm], ropes[i * 32:(i + 1) * 32, :], reads=(), writes=('ropeT',))
            dma(pb, p32, p32_d, reads=(), writes=('p32',))
            if True:
                emit_ln2(pb, nc, T, W, 'dram', 'ln', True)
            xnb = T['xnb']
            with _scope(pb) as ph:
                pP = _ps(ph, nc, 'pP', [32, TOK])
                pSW = _ps(ph, nc, 'pSW', [32, TOK])
                for src, wsrc in (('q', wq), ('k', wk)):
                    cq, sq = ('cosQ', 'sinQ') if src == 'q' else ('cosK', 'sinK')
                    for g in range(32):
                        gb = g % 2
                        dma(pb, wqst[gb], wsrc[g], reads=('wqb%dr' % gb,), writes=('wqst%d' % gb,))
                        vcopy(pb, wqb[gb], wqst[gb], reads=('wqst%d' % gb,), writes=('wqb%d' % gb, 'wqb%dr' % gb))
                        for c in range(NCH):
                            for i in range(4):
                                sl = slice(i * 512, (i + 1) * 512)
                                mm(pb, pP[:, sl], wqb[gb][:, c * 32:(c + 1) * 32],
                                   xnb[:, c * TOK + i * 512: c * TOK + (i + 1) * 512],
                                   reads=('wqb%d' % gb, 'xnb%d' % c), writes=('pP',),
                                   start=(c == 0), stop=(c == NCH - 1))
                        vcopy(pb, raw[gb], pP, reads=('pP',), writes=('raw%d' % gb,))
                        for i in range(4):
                            sl = slice(i * 512, (i + 1) * 512)
                            mm(pb, pSW[:, sl], p32, raw[gb][:, sl], reads=('raw%d' % gb, 'p32'),
                               writes=('pSW',), start=True, stop=True)
                        tt(pb, rt[gb], raw[gb], tabs[cq], A.mult, reads=('raw%d' % gb, 'ropeT'),
                           writes=('rt%d' % gb,))
                        tt(pb, ydst[gb], pSW, tabs[sq], A.mult, reads=('pSW', 'ropeT'), writes=('ydst%d' % gb,))
                        tt(pb, ydst[gb], ydst[gb], rt[gb], A.add, reads=('ydst%d' % gb, 'rt%d' % gb),
                           writes=('ydst%d' % gb,))
                        h = g % 16
                        which = src + ('1' if g < 16 else '2')
                        dma(pb, outs[which][:, h * TOK:(h + 1) * TOK], ydst[gb],
                            reads=('ydst%d' % gb,), writes=('o_' + which,))
            with _scope(pb) as ph:
                pV = [_ps(ph, nc, 'pV%d' % i, [128, TOK]) for i in range(2)]
                for i in range(16):
                    ib = i % 2
                    dma(pb, wvst[ib], wv[i], reads=('wvb%dr' % ib,), writes=('wvst%d' % ib,))
                    vcopy(pb, wvb[ib], wvst[ib], reads=('wvst%d' % ib,), writes=('wvb%d' % ib, 'wvb%dr' % ib))
                    p = pV[ib]
                    for c in range(NCH):
                        for j in range(4):
                            sl = slice(j * 512, (j + 1) * 512)
                            mm(pb, p[:, sl], wvb[ib][:, c * 128:(c + 1) * 128],
                               xnb[:, c * TOK + j * 512: c * TOK + (j + 1) * 512],
                               reads=('wvb%d' % ib, 'xnb%d' % c), writes=('pV%d' % ib,),
                               start=(c == 0), stop=(c == NCH - 1))
                    vcopy(pb, vout[ib], p, reads=('pV%d' % ib,), writes=('vout%d' % ib,))
                    dma(pb, o_vt[:, i * TOK:(i + 1) * TOK], vout[ib], reads=('vout%d' % ib,), writes=('o_vt',))
            pb.wait_all()
            pb.finalize(block)
    return nc


def build_l1scores():
    nc = bass.Bass("TRN2", target_bir_lowering=False)
    q1s_d = nc.dram_tensor("q1s", [32, 16 * TOK], BF16, kind="ExternalInput").ap()
    q2s_d = nc.dram_tensor("q2s", [32, 16 * TOK], BF16, kind="ExternalInput").ap()
    k1f_d = nc.dram_tensor("k1f", [32, 16 * 4096], BF16, kind="ExternalInput").ap()
    k2f_d = nc.dram_tensor("k2f", [32, 16 * 4096], BF16, kind="ExternalInput").ap()
    vf_d = nc.dram_tensor("vf", [2048, 4096], BF16, kind="ExternalInput").ap()
    identb_d = nc.dram_tensor("identb", [128, 128], BF16, kind="ExternalInput").ap()
    lam_d = nc.dram_tensor("lams", [4, 64], F32, kind="ExternalInput").ap()
    o_osc = nc.dram_tensor("o_osc", [1024, 4096], BF16, kind="ExternalOutput").ap()
    o_stats = nc.dram_tensor("o_stats", [1, 32], F32, kind="ExternalOutput").ap()
    with contextlib.ExitStack() as es:
        pb = PB(nc, es)
        T = {}
        T['onesb'] = _sb(es, nc, 'onesb', [128, 1], BF16)
        T['ones1'] = _sb(es, nc, 'ones1', [1, 128], BF16)
        T['ones1f'] = _sb(es, nc, 'ones1f', [1, 128], F32)
        T['identb'] = _sb(es, nc, 'identb', [128, 128], BF16)
        for i in range(2):
            T['k1h%d' % i] = _sb(es, nc, 'k1h%d' % i, [32, 4096], BF16)
            T['k2h%d' % i] = _sb(es, nc, 'k2h%d' % i, [32, 4096], BF16)
            T['q1h%d' % i] = _sb(es, nc, 'q1h%d' % i, [32, TOK], BF16)
            T['q2h%d' % i] = _sb(es, nc, 'q2h%d' % i, [32, TOK], BF16)
            T['vh%d' % i] = _sb(es, nc, 'vh%d' % i, [128, 4096], BF16)
            T['vtok%d' % i] = _sb(es, nc, 'vtok%d' % i, [128, 4096], BF16)
            T['comb%d' % i] = _sb(es, nc, 'comb%d' % i, [128, TOK], BF16)
            T['e1_%d' % i] = _sb(es, nc, 'e1_%d' % i, [128, 512], BF16)
            T['e2_%d' % i] = _sb(es, nc, 'e2_%d' % i, [128, 512], BF16)
        T['ct1'] = _sb(es, nc, 'ct1', [128, 512], F32)
        T['ct2'] = _sb(es, nc, 'ct2', [128, 512], F32)
        T['r1f'] = _sb(es, nc, 'r1f', [1, 512], F32)
        T['r2f'] = _sb(es, nc, 'r2f', [1, 512], F32)
        T['r1b'] = _sb(es, nc, 'r1b', [1, 512], BF16)
        T['r2b'] = _sb(es, nc, 'r2b', [1, 512], BF16)
        T['r1bc'] = _sb(es, nc, 'r1bc', [128, 512], BF16)
        T['r2bc'] = _sb(es, nc, 'r2bc', [128, 512], BF16)
        T['nlamcol'] = _sb(es, nc, 'nlamcol', [128, 1], F32)
        for nm in ('lq1', 'lk1', 'lq2', 'lk2'):
            T[nm] = _sb(es, nc, nm, [1, 64], F32)
        T['lrow'] = _sb(es, nc, 'lrow', [1, 64], F32)
        T['d1'] = _sb(es, nc, 'd1', [1, 1], F32)
        T['d2'] = _sb(es, nc, 'd2', [1, 1], F32)
        T['statcols'] = _sb(es, nc, 'statcols', [128, 32], F32)
        T['onesbf'] = _sb(es, nc, 'onesbf', [128, 1], F32)
        T['statrow'] = _sb(es, nc, 'statrow', [1, 32], F32)
        T['sqscr'] = _sb(es, nc, 'sqscr', [128, TOK], BF16)
        T['oscb'] = _sb(es, nc, 'oscb', [64, 4096], BF16)
        W = {'k1f': k1f_d, 'k2f': k2f_d, 'vf': vf_d, 'q1s': q1s_d, 'q2s': q2s_d,
             'lq1': lam_d[0:1, :], 'lk1': lam_d[1:2, :], 'lq2': lam_d[2:3, :], 'lk2': lam_d[3:4, :],
             'o_osc': o_osc, 'o_stats': o_stats}
        with nc.Block() as block:
            _init_ones(pb, T)
            memset(pb, T['onesbf'][:], 1.0, reads=(), writes=('onesbf',))
            dma(pb, T['identb'], identb_d, reads=(), writes=('identb',))
            memset(pb, T['statcols'][:], 0.0, reads=(), writes=('statcols',))
            emit_l1scores(pb, nc, es, T, W)
            pb.wait_all()
            pb.finalize(block)
    return nc


def build_wproj():
    nc = bass.Bass("TRN2", target_bir_lowering=False)
    xtin = nc.dram_tensor("xtin", [128, NCH * TOK], F32, kind="ExternalInput").ap()
    oscg_d = nc.dram_tensor("oscg", [2048, 4096], BF16, kind="ExternalInput").ap()
    statsg_d = nc.dram_tensor("statsg", [2, 32], F32, kind="ExternalInput").ap()
    wp_d = nc.dram_tensor("wp", [16, 128, 2048], F32, kind="ExternalInput").ap()
    alpha_d = nc.dram_tensor("alpha", [1, 1], F32, kind="ExternalInput").ap()
    mcol_d = nc.dram_tensor("mcol", [128, 2], F32, kind="ExternalInput").ap()
    o_xt = nc.dram_tensor("o_xt", [128, NCH * TOK], F32, kind="ExternalOutput").ap()
    with contextlib.ExitStack() as es:
        pb = PB(nc, es)
        T = {}
        T['ones1f'] = _sb(es, nc, 'ones1f', [1, 128], F32)
        T['st2'] = _sb(es, nc, 'st2', [1, 64], F32)
        T['st1'] = _sb(es, nc, 'st1', [1, 32], F32)
        T['mu'] = _sb(es, nc, 'mu', [1, 16], F32)
        T['msq'] = _sb(es, nc, 'msq', [1, 16], F32)
        T['vtmp'] = _sb(es, nc, 'vtmp', [1, 16], F32)
        T['al'] = _sb(es, nc, 'al', [1, 1], F32)
        T['mcols'] = _sb(es, nc, 'mcols', [128, 2], F32)
        T['cols'] = _sb(es, nc, 'cols', [128, 33], F32)
        T['oscin0'] = _sb(es, nc, 'oscin0', [128, 4096], BF16)
        T['oscin1'] = _sb(es, nc, 'oscin1', [128, 4096], BF16)
        T['rtmp'] = _sb(es, nc, 'rtmp', [128, TOK], F32)
        T['rtmp2'] = _sb(es, nc, 'rtmp2', [128, TOK], F32)
        T['rhsall'] = _sb(es, nc, 'rhsall', [128, 16 * TOK], BF16)
        T['wpst0'] = _sb(es, nc, 'wpst0', [128, TOK], F32)
        T['wpst1'] = _sb(es, nc, 'wpst1', [128, TOK], F32)
        T['wpall'] = _sb(es, nc, 'wpall', [128, 16 * TOK], BF16)
        T['gaf'] = _sb(es, nc, 'gaf', [128, TOK], F32)
        T['gvf'] = _sb(es, nc, 'gvf', [128, TOK], F32)
        T['xst0'] = _sb(es, nc, 'xst0', [128, TOK], F32)
        T['xst1'] = T['xst0']
        W = {'statsg': statsg_d, 'oscg': oscg_d, 'wp': [wp_d[i] for i in range(16)],
             'alpha': alpha_d, 'mcol': mcol_d, 'xtin': xtin, 'o_xt': o_xt}
        with nc.Block() as block:
            pb.emit('gpsimd', lambda E: E.memset(T['ones1f'][:], 1.0), reads=(), writes=('ones1f',),
                    sem='pool', inc=1)
            emit_wproj(pb, nc, es, T, W)
            pb.wait_all()
            pb.finalize(block)
    return nc



_PROGS = {}

def _prog(name, fn, *a):
    if name not in _PROGS:
        _PROGS[name] = fn(*a)
    return _PROGS[name]


def _host_prep(inp):
    H = {}
    f32 = np.float32
    wgT = np.ascontiguousarray(inp['L0_ffn_w_gate'].T)
    H['wg'] = np.ascontiguousarray(wgT.reshape(8, 128, 24, 128).transpose(2, 1, 0, 3)).reshape(24, 128, 1024)
    wuT = np.ascontiguousarray(inp['L0_ffn_w_up'].T)
    H['wu'] = np.ascontiguousarray(wuT.reshape(8, 128, 24, 128).transpose(2, 1, 0, 3)).reshape(24, 128, 1024)
    wdT = np.ascontiguousarray(inp['L0_ffn_w_down'].T)
    H['wd'] = np.ascontiguousarray(wdT.reshape(24, 128, 8, 128).transpose(2, 1, 0, 3)).reshape(8, 128, 3072)
    H['wdkvT'] = np.ascontiguousarray(
        inp['L0_attn_W_dkv'].T.reshape(8, 128, 128).transpose(1, 0, 2)).reshape(128, 1024).astype(f32)
    H['wdqT'] = np.ascontiguousarray(
        inp['L0_attn_W_dq'].T.reshape(8, 128, 128).transpose(1, 0, 2)).reshape(128, 1024).astype(f32)
    H['wuqT'] = np.ascontiguousarray(inp['L0_attn_W_uq'].T).astype(f32)
    H['wdq'] = np.ascontiguousarray(inp['L0_attn_W_dq']).astype(f32)
    H['wuk'] = np.ascontiguousarray(
        inp['L0_attn_W_uk'].reshape(8, 128, 128).transpose(1, 0, 2)).reshape(128, 1024).astype(f32)
    H['wuv'] = np.ascontiguousarray(
        inp['L0_attn_W_uv'].reshape(8, 128, 128).transpose(1, 0, 2)).reshape(128, 1024).astype(f32)
    H['woT'] = np.ascontiguousarray(
        inp['L0_attn_W_o'].T.reshape(8, 128, 1024).transpose(1, 0, 2)).reshape(128, 8192).astype(f32)
    H['identb'] = np.eye(128, dtype=BF)
    H['identf'] = np.eye(128, dtype=f32)
    tri = np.tril(np.ones((128, 128), f32))
    ones = np.ones((128, 128), f32)
    zer = np.zeros((128, 128), f32)
    H['masks0'] = np.concatenate([tri, zer], axis=0).astype(BF)
    H['masks1'] = np.concatenate([ones, tri], axis=0).astype(BF)
    H['w1'] = np.ascontiguousarray(
        inp['L1_ffn_W1'].reshape(8, 8, 128, 32, 128).transpose(0, 3, 2, 1, 4)).reshape(256, 128, 1024).astype(f32)
    H['w2'] = np.ascontiguousarray(
        inp['L1_ffn_W2'].reshape(8, 32, 128, 8, 128).transpose(0, 3, 2, 1, 4)).reshape(64, 128, 4096).astype(f32)
    H['gwT'] = np.ascontiguousarray(
        inp['L1_ffn_gate_w'].T.reshape(8, 128, 8).transpose(1, 0, 2)).reshape(128, 64).astype(f32)
    H['b1s'] = np.ascontiguousarray(
        inp['L1_ffn_b1'].reshape(8, 32, 128).transpose(2, 0, 1)).reshape(128, 256).astype(f32)
    H['b2s'] = np.ascontiguousarray(
        inp['L1_ffn_b2'].reshape(8, 8, 128).transpose(2, 0, 1)).reshape(128, 64).astype(f32)
    H['rmsw'] = np.ascontiguousarray(inp['L1_ffn_rms_w'].reshape(8, 128).T).astype(f32)
    H['wq'] = np.ascontiguousarray(
        inp['L1_attn_Wq'].T.reshape(8, 128, 32, 32).transpose(2, 1, 0, 3)).reshape(32, 128, 256).astype(f32)
    H['wk'] = np.ascontiguousarray(
        inp['L1_attn_Wk'].T.reshape(8, 128, 32, 32).transpose(2, 1, 0, 3)).reshape(32, 128, 256).astype(f32)
    H['wv'] = np.ascontiguousarray(
        inp['L1_attn_Wv'].T.reshape(8, 128, 16, 128).transpose(2, 1, 0, 3)).reshape(16, 128, 1024).astype(f32)
    H['lams'] = np.stack([np.asarray(inp['L1_attn_lambda_q1'], f32),
                          np.asarray(inp['L1_attn_lambda_k1'], f32),
                          np.asarray(inp['L1_attn_lambda_q2'], f32),
                          np.asarray(inp['L1_attn_lambda_k2'], f32)]).reshape(4, 64)
    H['alpha'] = np.asarray(inp['L1_attn_alpha'], f32).reshape(1, 1)
    perm = np.empty(2048, np.int64)
    idx = 0
    for h in range(16):
        for rank in range(2):
            for t in range(16):
                for jj in range(4):
                    perm[idx] = h * 128 + 8 * t + 4 * rank + jj
                    idx += 1
    wpT = np.ascontiguousarray(inp['L1_attn_Wproj'].T)[perm, :]
    H['wp'] = np.ascontiguousarray(wpT).reshape(16, 128, 2048).astype(f32)
    i16 = np.arange(16, dtype=np.float64)
    inv_freq = 1.0 / 10000.0 ** (i16 / 256.0)
    for par in range(2):
        tloc = np.arange(TOK)
        pos = 128 * (2 * (tloc // 128) + par) + tloc % 128
        fr = np.outer(pos.astype(np.float64), inv_freq)
        cos = np.cos(fr).astype(f32).astype(BF).astype(f32)
        sin = np.sin(fr).astype(f32).astype(BF).astype(f32)
        def mk(tab, sign, scale):
            out = np.empty((32, TOK), f32)
            for r in range(32):
                out[r] = tab[:, r % 16] * (sign if r >= 16 else 1.0) * scale
            return out.astype(BF)
        cQ = mk(cos, 1.0, 0.125)
        sQ = np.empty((32, TOK), f32)
        for r in range(32):
            sQ[r] = (-sin[:, r % 16] if r < 16 else sin[:, r % 16]) * 0.125
        sQ = sQ.astype(BF)
        cK = mk(cos, 1.0, 1.0)
        sK = np.empty((32, TOK), f32)
        for r in range(32):
            sK[r] = (-sin[:, r % 16] if r < 16 else sin[:, r % 16])
        sK = sK.astype(BF)
        H['ropes%d' % par] = np.concatenate([cQ, sQ, cK, sK], axis=0)
    p32 = np.zeros((32, 32), f32)
    for r in range(32):
        p32[(r + 16) % 32, r] = 1.0
    H['p32'] = p32.astype(BF)
    for par in range(2):
        m = 1.0 if par == 0 else 0.0
        H['mcol%d' % par] = np.concatenate(
            [np.full((128, 1), m, f32), np.full((128, 1), 1.0 - m, f32)], axis=1)
    return H


def _run(progname, buildfn, in_maps, *bargs):
    ncp = _prog(progname, buildfn, *bargs)
    res = run_bass_kernel_spmd(ncp, in_maps, core_ids=list(range(NCORE)))
    return res.results


def _shard_x(x):
    xts = []
    for c in range(NCORE):
        b, par = c // 2, c % 2
        xb = np.ascontiguousarray(x[b].reshape(32, 128, 1024)[par::2]).reshape(TOK, 1024)
        xT = np.ascontiguousarray(xb.T)
        xts.append(np.ascontiguousarray(xT.reshape(8, 128, TOK).transpose(1, 0, 2)).reshape(128, NCH * TOK))
    return xts


def _unshard_x(xts):
    out = np.empty((4, 4096, 1024), np.float32)
    for c in range(NCORE):
        b, par = c // 2, c % 2
        xT = np.ascontiguousarray(np.asarray(xts[c]).reshape(128, 8, TOK).transpose(1, 0, 2)).reshape(1024, TOK)
        toks = np.ascontiguousarray(xT.T)
        out[b].reshape(32, 128, 1024)[par::2] = toks.reshape(16, 128, 1024)
    return out


def kernel(**inp):
    inp = {k: np.asarray(v) for k, v in inp.items()}
    H = _host_prep(inp)
    xts = _shard_x(np.asarray(inp['L0_attn_x'], np.float32))

    ims = [{'xtin': xts[c], 'wg': H['wg'], 'wu': H['wu'], 'wd': H['wd'], 'wdkvT': H['wdkvT']}
           for c in range(NCORE)]
    r = _run('A', build_l0ffn, ims, True)
    xts = [r[c]['o_xt'] for c in range(NCORE)]
    ckv = [np.asarray(r[c]['o_ckv']) for c in range(NCORE)]

    ims = []
    for c in range(NCORE):
        pe = (c // 2) * 2
        ckvf = np.ascontiguousarray(np.concatenate([ckv[pe], ckv[pe + 1]], axis=1))
        ims.append({'xtin': xts[c], 'ckvf': ckvf, 'masks': H['masks%d' % (c % 2)],
                    'identb': H['identb'], 'wdqT': H['wdqT'], 'wuqT': H['wuqT'], 'wdq': H['wdq'],
                    'wuk': H['wuk'], 'wuv': H['wuv'], 'woT': H['woT']})
    r = _run('B', build_l0attn, ims)
    xts = [r[c]['o_xt'] for c in range(NCORE)]

    ims = [{'xtin': xts[c], 'wg': H['wg'], 'wu': H['wu'], 'wd': H['wd']} for c in range(NCORE)]
    r = _run('C', build_l0ffn, ims, False)
    xts = [r[c]['o_xt'] for c in range(NCORE)]

    def l1ffn_maps(xts):
        return [{'xtin': xts[c], 'w1': H['w1'], 'w2': H['w2'], 'gwT': H['gwT'],
                 'b1s': H['b1s'], 'b2s': H['b2s'], 'rmsw': H['rmsw'], 'identf': H['identf']}
                for c in range(NCORE)]
    r = _run('D', build_l1ffn, l1ffn_maps(xts))
    xts = [r[c]['o_xt'] for c in range(NCORE)]
    x4 = xts

    ims = [{'xtin': x4[c], 'wq': H['wq'], 'wk': H['wk'], 'wv': H['wv'],
            'ropes': H['ropes%d' % (c % 2)], 'p32': H['p32']} for c in range(NCORE)]
    r = _run('E', build_l1qkv, ims)
    q1 = [np.asarray(r[c]['o_q1s']) for c in range(NCORE)]
    q2 = [np.asarray(r[c]['o_q2s']) for c in range(NCORE)]
    k1 = [np.asarray(r[c]['o_k1s']) for c in range(NCORE)]
    k2 = [np.asarray(r[c]['o_k2s']) for c in range(NCORE)]
    vt = [np.asarray(r[c]['o_vt']) for c in range(NCORE)]

    ims = []
    for c in range(NCORE):
        pe = (c // 2) * 2
        k1f = np.ascontiguousarray(np.concatenate(
            [k1[pe].reshape(32, 16, TOK), k1[pe + 1].reshape(32, 16, TOK)], axis=2)).reshape(32, 16 * 4096)
        k2f = np.ascontiguousarray(np.concatenate(
            [k2[pe].reshape(32, 16, TOK), k2[pe + 1].reshape(32, 16, TOK)], axis=2)).reshape(32, 16 * 4096)
        vloc = [np.ascontiguousarray(vt[p].reshape(128, 16, TOK).transpose(1, 0, 2)).reshape(2048, TOK)
                for p in (pe, pe + 1)]
        vf = np.ascontiguousarray(np.concatenate(vloc, axis=1))
        ims.append({'q1s': q1[c], 'q2s': q2[c], 'k1f': k1f, 'k2f': k2f, 'vf': vf,
                    'identb': H['identb'], 'lams': H['lams']})
    r = _run('F', build_l1scores, ims)
    osc = [np.asarray(r[c]['o_osc']) for c in range(NCORE)]
    stats = [np.asarray(r[c]['o_stats']) for c in range(NCORE)]

    ims = []
    for c in range(NCORE):
        pe = (c // 2) * 2
        oscg = np.ascontiguousarray(np.concatenate([osc[pe], osc[pe + 1]], axis=0))
        statsg = np.ascontiguousarray(np.concatenate([stats[pe], stats[pe + 1]], axis=0))
        ims.append({'xtin': x4[c], 'oscg': oscg, 'statsg': statsg, 'wp': H['wp'],
                    'alpha': H['alpha'], 'mcol': H['mcol%d' % (c % 2)]})
    r = _run('G', build_wproj, ims)
    xts = [r[c]['o_xt'] for c in range(NCORE)]

    r = _run('H', build_l1ffn, l1ffn_maps(xts))
    xts = [r[c]['o_xt'] for c in range(NCORE)]

    return _unshard_x(xts)
